# version 03 (final)
# baseline (speedup 1.0000x reference)
import numpy as np
import ml_dtypes
import concourse.bass as bass
import concourse.mybir as mybir
from concourse.bass_utils import run_bass_kernel_spmd

F32 = mybir.dt.float32
BF16 = mybir.dt.bfloat16
U16 = mybir.dt.uint16
AF = mybir.ActivationFunctionType
ALU = mybir.AluOpType
AXX = mybir.AxisListType.X

NCORES = 8
SCAL_VALUES = [1e-5, 1.6487212707001282, 0.00064, 1e-12, 0.5, 2.0, -1.0, 128.0]
CM = 1024
NH = 16
ND = 64
LCH = 128
BLK = 16
NBC = LCH // BLK

def ts(i, n):
    return slice(i * n, (i + 1) * n)


def host_consts(dt_np=np.float32):
    c = {}
    c["c_ident"] = np.eye(128, dtype=np.float32)
    c["c_ident_bf"] = np.eye(128, dtype=ml_dtypes.bfloat16)
    s = np.arange(128)[:, None]
    t = np.arange(128)[None, :]
    sameblk = (s // BLK) == (t // BLK)
    c["c_mask_st_strict"] = (sameblk & (s < t)).astype(np.float32)
    c["c_mask_st_incl"] = (sameblk & (s <= t)).astype(np.float32)
    c["c_mask_st_band1"] = (((t // BLK) - (s // BLK)) == 1).astype(np.float32)
    c["c_mask_ts_strict"] = (sameblk & (s > t)).astype(np.float32)
    c["c_mask_ts_band1"] = c["c_mask_st_band1"].T.copy()
    bones = np.zeros((8, 128, 16), dtype=np.float32)
    for p in range(8):
        bones[p, 0:64, 2 * p] = 1.0
        bones[p, 64:128, 2 * p + 1] = 1.0
    c["c_bones"] = bones
    c["c_bcast"] = np.transpose(bones, (0, 2, 1)).copy()
    tail = np.zeros((128, 16), dtype=np.float32)
    tail[112:128] = np.eye(16)
    c["c_tail"] = tail
    c["c_ones_row"] = np.ones((1, 512), dtype=np.float32)
    c["c_scal"] = np.array(SCAL_VALUES, dtype=np.float32).reshape(1, -1)
    c["c_mskb1c"] = np.concatenate([c["c_mask_st_band1"], c["c_mask_ts_band1"],
                                    c["c_mask_st_band1"], c["c_mask_st_band1"]], axis=1)
    c["c_mskd0c"] = np.concatenate([c["c_mask_st_strict"], c["c_mask_ts_strict"],
                                    c["c_mask_st_strict"], c["c_mask_st_incl"]], axis=1)
    return c


def f32c(x):
    return np.ascontiguousarray(np.asarray(x, np.float32))



PHYS = {"qg": "g", "qs": "qs", "qa": "s"}


class Plan:

    QPOOL = 16

    def __init__(self, nc):
        self.nc = nc
        self.items = []
        self.sem = {}
        self.cnt = {}
        self.hw = {}
        self.wr = {}
        self.rd = {}
        self.pe_rd = []
        self.pe_wr = []
        self._last_pe_item = None
        for e in ("t", "v", "s", "g"):
            self.sem[e] = nc.alloc_semaphore("pl_" + e)
            self.cnt[e] = 0
        self.qrr = {}
        for q in ("qs", "qg", "qa"):
            self.qrr[q] = 0
            for i in range(self.QPOOL):
                e = q + str(i)
                self.sem[e] = nc.alloc_semaphore("pl_" + e)
                self.cnt[e] = 0

    def _phys(self, ek):
        return PHYS.get(ek, ek)

    def _mkwaits(self, ek, ins_r, outs_r):
        pe = self._phys(ek)
        waits = []
        def need(sn, val):
            if val <= 0:
                return
            k = (pe, sn)
            if self.hw.get(k, 0) >= val:
                return
            waits.append((sn, val))
            self.hw[k] = val
        for r in ins_r:
            w = self.wr.get(r)
            if w:
                need(w[0], w[1])
        for r in outs_r:
            w = self.wr.get(r)
            if w:
                need(w[0], w[1])
            for s, c in self.rd.get(r, {}).items():
                need(s, c)
        return waits

    def _post(self, sem_name, newcnt, ins_r, outs_r):
        for r in ins_r:
            self.rd.setdefault(r, {})[sem_name] = newcnt
        for r in outs_r:
            self.wr[r] = (sem_name, newcnt)
            self.rd[r] = {}

    def _maybe_flush_pe(self, ins, outs):
        regs = set(ins) | set(outs)
        if regs & set(self.pe_wr) or set(outs) & set(self.pe_rd):
            self.pe_flush()

    def op(self, ek, f, ins=(), outs=()):
        assert ek in ("v", "s", "g")
        self._maybe_flush_pe(ins, outs)
        waits = self._mkwaits(ek, ins, outs)
        self.cnt[ek] += 1
        self.items.append((self._phys(ek), waits, f, [(ek, 1)]))
        self._post(ek, self.cnt[ek], ins, outs)

    def pe(self, f, ins=(), outs=()):
        waits = self._mkwaits("t", ins, outs)
        item = ["t", waits, f, []]
        self.items.append(item)
        self.pe_rd.extend(ins)
        self.pe_wr.extend(outs)
        self._last_pe_item = item

    def pe_flush(self):
        if not (self.pe_rd or self.pe_wr):
            return
        self.cnt["t"] += 1
        self._last_pe_item[3].append(("t", 1))
        c = self.cnt["t"]
        for r in self.pe_rd:
            self.rd.setdefault(r, {})["t"] = c
        for r in self.pe_wr:
            self.wr[r] = ("t", c)
            self.rd[r] = {}
        self.pe_rd = []
        self.pe_wr = []

    def dma(self, q, out_ap, in_ap, ins=(), outs=(), **kw):
        assert q in ("qs", "qg", "qa")
        self._maybe_flush_pe(ins, outs)
        waits = self._mkwaits(q, ins, outs)
        sname = q + str(self.qrr[q] % self.QPOOL)
        self.qrr[q] += 1
        if self.cnt[sname] > 0:
            pe_ = self._phys(q)
            k = (pe_, sname)
            if self.hw.get(k, 0) < self.cnt[sname]:
                waits.append((sname, self.cnt[sname]))
                self.hw[k] = self.cnt[sname]
        self.cnt[sname] += 16
        def f(eng, out_ap=out_ap, in_ap=in_ap, kw=kw):
            return eng.dma_start(out=out_ap, in_=in_ap, **kw)
        self.items.append((self._phys(q), waits, f, [(sname, 16)]))
        self._post(sname, self.cnt[sname], ins, outs)

    def wait_on(self, ek, regions):
        self._maybe_flush_pe(regions, ())
        waits = self._mkwaits(ek, regions, ())
        for sn, val in waits:
            def f(eng, sn=sn, val=val):
                return eng.wait_ge(self.sem[sn], val)
            self.items.append((self._phys(ek), [], f, []))

    def barrier(self):
        self.pe_flush()
        engines = ("t", "v", "s", "g", "qs")
        cur = {e: n for e, n in self.cnt.items()}
        for ewait in engines:
            pe_ = self._phys(ewait)
            for esrc, c in cur.items():
                if self._phys(esrc) == pe_ or c == 0:
                    continue
                k = (pe_, esrc)
                if self.hw.get(k, 0) >= c:
                    continue
                self.hw[k] = c
                def f(eng, esrc=esrc, c=c):
                    return eng.wait_ge(self.sem[esrc], c)
                self.items.append((pe_, [], f, []))

    def emit(self):
        nc = self.nc
        with nc.Block() as block:
            def mkbody(pek):
                def body(eng):
                    for (pe_, waits, f, incs) in self.items:
                        if pe_ != pek:
                            continue
                        for sn, val in waits:
                            eng.wait_ge(self.sem[sn], val)
                        inst = f(eng)
                        for sn, amt in incs:
                            inst.then_inc(self.sem[sn], amt)
                return body
            block.tensor(mkbody("t"))
            block.vector(mkbody("v"))
            block.scalar(mkbody("s"))
            block.gpsimd(mkbody("g"))
            block.sync(mkbody("qs"))

class Carver:
    def __init__(self, nc, total_bytes):
        self.t = nc.alloc_sbuf_tensor("ARENA", [128, total_bytes], mybir.dt.uint8)
        self.off = 0
        self.total = total_bytes

    def carve(self, shape, dtype):
        esz = mybir.dt.size(dtype)
        n = int(np.prod(shape[1:])) * esz
        off = (self.off + 63) & ~63
        assert off + n <= self.total, f"SBUF arena overflow: {off}+{n} > {self.total}"
        self.off = off + n
        ap = self.t.ap()[:, off:off + n].bitcast(dtype)
        if len(shape) == 2:
            ap = ap[0:shape[0], 0:shape[1]]
        elif len(shape) == 3:
            ap = ap.rearrange("p (a b) -> p a b", a=shape[1], b=shape[2])[0:shape[0]]
        elif len(shape) == 4:
            ap = ap.rearrange("p (a b c) -> p a b c", a=shape[1], b=shape[2], c=shape[3])[0:shape[0]]
        else:
            raise ValueError(shape)
        return ap


class Ctx(dict):
    __getattr__ = dict.__getitem__
    __setattr__ = dict.__setitem__


def build_core_program(T=4096, TCA=256, TCB=256, TCF=512, dbg=(), solve_levels=4):
    nc = bass.Bass("TRN2", target_bir_lowering=False)
    c = Ctx()
    c.T, c.TCA, c.TCB, c.TCF, c.solve_levels = T, TCA, TCB, TCF, solve_levels
    c.dbg = set(nm for nm, _ in dbg)

    c.x_in = nc.dram_tensor("x", [T, CM], F32, kind="ExternalInput").ap()
    c.out_t = nc.dram_tensor("out", [T, CM], F32, kind="ExternalOutput").ap()
    wspec = [
        ("time_maa_x", [1, 1, CM]), ("time_maa_rg", [1, 1, CM]), ("time_maa_wa", [1, 1, CM]),
        ("time_maa_k", [1, 1, CM]), ("time_maa_v", [1, 1, CM]),
        ("time_maa_w1", [CM, 128]), ("time_maa_w2", [4, 32, CM]),
        ("time_decay", [1, 1, CM]), ("time_decay_w1", [CM, 64]), ("time_decay_w2", [64, CM]),
        ("time_faaaa", [1, 1, NH, ND]), ("time_aaaaa", [1, 1, CM]),
        ("time_aaa_w1", [CM, 16]), ("time_aaa_w2", [16, CM]),
        ("time_kkk_w1", [CM, 16]), ("time_kkk_w2", [16, CM]),
        ("gate_w1", [CM, 128]), ("gate_w2", [128, CM]),
        ("ma_w1", [CM, 16]), ("ma_w2", [16, CM]),
        ("time_misc_a", [1, 1, CM]), ("mk_w1", [CM, 16]), ("mk_w2", [16, CM]),
        ("time_misc_k", [1, 1, CM]),
        ("w_receptance", [CM, CM]), ("w_key", [CM, CM]), ("w_value", [CM, CM]),
        ("w_output", [CM, CM]), ("ln_w", [CM]), ("ln_b", [CM]),
        ("W_router", [CM, 4]), ("W1", [4, CM, 4096]), ("b1", [4, 4096]),
        ("Wo", [4096, CM]), ("bo", [CM]), ("ln_gamma", [CM]), ("ln_beta", [CM]),
    ]
    c.W = {}
    for nm, shp in wspec:
        c.W[nm] = nc.dram_tensor(nm, shp, F32, kind="ExternalInput").ap()
    c.CONSTS = {}
    for nm, arr in host_consts().items():
        c.CONSTS[nm] = nc.dram_tensor(nm, list(arr.shape), F32, kind="ExternalInput").ap()

    c.sp = {}
    for nm in ("sp_r", "sp_k2", "sp_kkn", "sp_asig"):
        c.sp[nm] = nc.dram_tensor(nm, [8, 128, T], BF16).ap()
    for nm in ("sp_v", "sp_g"):
        c.sp[nm] = nc.dram_tensor(nm, [T, CM], BF16).ap()
    c.sp["sp_wneg"] = nc.dram_tensor("sp_wneg", [8, 128, T], F32).ap()
    c.sp["sp_term"] = nc.dram_tensor("sp_term", [T, 16], F32).ap()
    c.sp["x1s"] = nc.dram_tensor("x1s", [T, CM], F32).ap()
    c.DBG = {}
    for nm, shp in dbg:
        c.DBG[nm] = nc.dram_tensor("dbg_" + nm, list(shp), F32, kind="ExternalOutput").ap()

    c.PS = [nc.alloc_psum_tensor(f"ps{i}", [128, 512], F32).ap() for i in range(8)]
    c.PSN = [f"PS{i}" for i in range(8)]

    P = Plan(nc)
    car = Carver(nc, 212300)
    c.car = car

    c.IDENT = car.carve([128, 128], F32)
    c.IDENTB = car.carve([128, 128], BF16)
    c.MSK_ST_S = car.carve([128, 128], F32)
    c.MSK_ST_B1 = car.carve([128, 128], F32)
    c.MSK_TS_S = car.carve([128, 128], F32)
    c.MSK_TS_B1 = car.carve([128, 128], F32)
    c.BONES = car.carve([128, 8, 16], BF16)
    c.BCAST = car.carve([16, 8, 128], BF16)
    c.BONESF = car.carve([128, 8, 16], BF16)
    c.TAILX = car.carve([128, 16], BF16)
    c.ONESROW = car.carve([1, 512], BF16)
    c.CCOLS = car.carve([128, 8, 8], F32)
    c.CMIX = car.carve([128, 8, 4], F32)
    c.B1COL = car.carve([128, 4, 32], F32)
    c.GAMR = car.carve([128, 1024], F32)
    c.BETR = car.carve([128, 1024], F32)
    c.BOROW = car.carve([1, 1024], BF16)
    c.WROUT = car.carve([128, 8, 4], BF16)
    c.MSKB1C = car.carve([128, 512], F32)
    c.MSKD0CB = car.carve([128, 512], BF16)
    c.MSK_ST_IB = car.carve([128, 128], BF16)
    c.LNWR = car.carve([128, 1024], F32)
    c.LNBR = car.carve([128, 1024], F32)
    c.SCALS = car.carve([128, len(SCAL_VALUES)], F32)
    for i, v in enumerate(SCAL_VALUES):
        nc.const_aps.register(F32, float(v), c.SCALS[:, i:i + 1])
    c.const_end = car.off

    emit_weight_prep(P, c)
    emit_pass1a(P, c)
    emit_pass1b(P, c)
    emit_pass2(P, c)
    P.barrier()
    P.emit()
    return nc, c

def emit_weight_prep(P, c):
    nc, car = P.nc, c.car
    CN = c.CONSTS
    P.dma("qs", c.IDENT[:, :], CN["c_ident"][:, :], outs=["IDENT"])
    P.dma("qg", c.IDENTB[:, :], CN["c_ident"][:, :], outs=["IDENTB"])
    P.dma("qs", c.MSK_ST_S[:, :], CN["c_mask_st_strict"][:, :], outs=["MSK_ST_S"])
    P.dma("qs", c.MSK_ST_B1[:, :], CN["c_mask_st_band1"][:, :], outs=["MSK_ST_B1"])
    P.dma("qs", c.MSK_TS_S[:, :], CN["c_mask_ts_strict"][:, :], outs=["MSK_TS_S"])
    P.dma("qs", c.MSK_TS_B1[:, :], CN["c_mask_ts_band1"][:, :], outs=["MSK_TS_B1"])
    P.dma("qg", c.BONES[:, :, :], CN["c_bones"].rearrange("a p b -> p a b"), outs=["BONES"])
    P.dma("qg", c.BCAST[:, :, :], CN["c_bcast"].rearrange("a h p -> h a p"), outs=["BCAST"])
    P.dma("qg", c.TAILX[:, :], CN["c_tail"][:, :], outs=["TAILX"])
    P.dma("qg", c.ONESROW[:, :], CN["c_ones_row"][:, :], outs=["ONESROW"])
    P.dma("qg", c.BOROW[0:1, :], c.W["bo"].unsqueeze(0), outs=["BOROW"])
    P.dma("qs", c.GAMR[:, :], c.W["ln_gamma"].unsqueeze(0).broadcast_to([128, CM]), outs=["GAMR"])
    P.dma("qs", c.BETR[:, :], c.W["ln_beta"].unsqueeze(0).broadcast_to([128, CM]), outs=["BETR"])
    P.dma("qg", c.WROUT[:, :, :], c.W["W_router"].rearrange("(a p) f -> p a f", p=128), outs=["WROUT"])
    P.dma("qs", c.SCALS[:, :], CN["c_scal"].broadcast_to([128, len(SCAL_VALUES)]), outs=["SCALS"])
    P.dma("qs", c.MSKB1C[:, :], CN["c_mskb1c"][:, :], outs=["MSKB1C"])
    P.dma("qg", c.MSKD0CB[:, :], CN["c_mskd0c"][:, :], outs=["MSKD0CB"])
    P.dma("qg", c.MSK_ST_IB[:, :], CN["c_mask_st_incl"][:, :], outs=["MSK_ST_IB"])
    P.dma("qs", c.LNWR[:, :], c.W["ln_w"].unsqueeze(0).broadcast_to([128, CM]), outs=["LNWR"])
    P.dma("qs", c.LNBR[:, :], c.W["ln_b"].unsqueeze(0).broadcast_to([128, CM]), outs=["LNBR"])

    VEC = car.carve([16, 1024], F32)
    P.op("v", lambda v: v.memset(VEC[:, :], 0.0), outs=["VEC"])
    rows = [("time_maa_x", 0), ("time_decay", 1), ("time_aaaaa", 2), ("time_misc_a", 3),
            ("time_misc_k", 4), ("ln_w", 5), ("ln_b", 6), ("time_faaaa", 7),
            ("time_maa_rg", 8), ("time_maa_wa", 9), ("time_maa_k", 10), ("time_maa_v", 11)]
    for nm, r in rows:
        ap = c.W[nm]
        flat = ap.rearrange("a b f -> a (b f)") if len(ap.shape) == 3 else (
            ap.rearrange("a b h n -> a (b h n)") if len(ap.shape) == 4 else ap.unsqueeze(0))
        P.dma("qs", VEC[r:r + 1, :], flat, outs=["VEC"])
    COLS = car.carve([128, 8, 16], F32)
    c.COLS = COLS
    for ch in range(8):
        pbank = c.PS[ch % 2]
        P.pe(lambda t, ch=ch, pbank=pbank: t.transpose(pbank[:, 0:16], VEC[:, ts(ch, 128)], c.IDENT[0:16, 0:16]),
             ins=["VEC", "IDENT"], outs=[c.PSN[ch % 2]])
        P.op("s", lambda s, ch=ch, pbank=pbank: s.copy(COLS[:, ch, :], pbank[:, 0:16]),
             ins=[c.PSN[ch % 2]], outs=["COLS"])
    P.op("v", lambda v: v.tensor_scalar(COLS[:, :, 1], COLS[:, :, 1], -1.0, 0.5, ALU.mult, ALU.add),
         ins=["COLS"], outs=["COLS"])
    for ch in range(8):
        P.op("v", lambda v, ch=ch: v.tensor_scalar_mul(c.BONESF[:, ch, :], c.BONES[:, ch, :], COLS[:, ch, 7:8]),
             ins=["BONES", "COLS"], outs=["BONESF"])

    B1FL = car.carve([128, 128], F32)
    P.dma("qs", B1FL[:, :], c.W["b1"].rearrange("e i -> (e i)").rearrange("(r p) -> r p", p=128), outs=["B1FL"])
    P.pe(lambda t: t.transpose(c.PS[2][:, 0:128], B1FL[:, :], c.IDENT[:, :]), ins=["B1FL", "IDENT"], outs=[c.PSN[2]])
    P.op("s", lambda s: s.copy(c.B1COL.rearrange("p a b -> p (a b)"), c.PS[2][:, 0:128]),
         ins=[c.PSN[2]], outs=["B1COL"])

    c.MAAW1 = car.carve([128, 8, 128], BF16)
    P.dma("qg", c.MAAW1[:, :, :], c.W["time_maa_w1"].rearrange("(a p) f -> p a f", p=128), outs=["MAAW1"])
    c.W2ALL = car.carve([128, 1024], BF16)
    P.dma("qg", c.W2ALL[:, :], c.W["time_maa_w2"].rearrange("f d c -> (f d) c"), outs=["W2ALL"])
    c.DW1 = car.carve([128, 8, 64], BF16)
    P.dma("qg", c.DW1[:, :, :], c.W["time_decay_w1"].rearrange("(a p) f -> p a f", p=128), outs=["DW1"])
    c.DW2 = car.carve([64, 1024], BF16)
    P.dma("qg", c.DW2[:, :], c.W["time_decay_w2"][:, :], outs=["DW2"])
    c.GW1 = car.carve([128, 8, 128], BF16)
    P.dma("qg", c.GW1[:, :, :], c.W["gate_w1"].rearrange("(a p) f -> p a f", p=128), outs=["GW1"])
    c.GW2 = car.carve([128, 1024], BF16)
    P.dma("qg", c.GW2[:, :], c.W["gate_w2"][:, :], outs=["GW2"])
    for nm, w1, w2 in (("A", "time_aaa_w1", "time_aaa_w2"), ("KK", "time_kkk_w1", "time_kkk_w2"),
                       ("MA", "ma_w1", "ma_w2"), ("MK", "mk_w1", "mk_w2")):
        t1 = car.carve([128, 8, 16], BF16)
        P.dma("qg", t1[:, :, :], c.W[w1].rearrange("(a p) f -> p a f", p=128), outs=[nm + "1"])
        t2 = car.carve([16, 1024], BF16)
        P.dma("qg", t2[:, :], c.W[w2][:, :], outs=[nm + "2"])
        c["L" + nm + "1"], c["L" + nm + "2"] = t1, t2

    WNAT = car.carve([128, 1024], BF16)
    c.after_wnat = car.off
    for wnm, key in (("w_output", "WOT"), ("w_receptance", "WRT"), ("w_key", "WKT"), ("w_value", "WVT")):
        WT = car.carve([128, 8, 1024], BF16)
        c[key] = WT
        if key == "WOT":
            c.after_wot = car.off
        for fc in range(8):
            P.dma("qg", WNAT[:, :], c.W[wnm][ts(fc, 128), :], ins=[], outs=["WNAT"])
            for cc4 in range(2):
                pb = c.PS[3 + cc4]
                pbb = pb.bitcast(BF16)
                for j in range(4):
                    cc = cc4 * 4 + j
                    P.pe(lambda t, cc=cc, j=j, pbb=pbb: t.transpose(
                        pbb[:, ts(j, 128)], WNAT[:, ts(cc, 128)], c.IDENTB[:, :]),
                        ins=["WNAT", "IDENTB"], outs=[c.PSN[3 + cc4]])
                P.op("s", lambda s, WT=WT, fc=fc, cc4=cc4, pbb=pbb: s.copy(
                    WT[:, ts(cc4 * 4, 4).start:ts(cc4 * 4 + 3, 4).stop, ts(fc, 128)].rearrange("p a b -> p (a b)") if False else
                    WT[:, cc4 * 4:cc4 * 4 + 4, ts(fc, 128)], pbb[:, 0:512].rearrange("p (a b) -> p a b", a=4)),
                    ins=[c.PSN[3 + cc4]], outs=[key])
    P.barrier()

def emit_pass1a(P, c):
    nc, car = P.nc, c.car
    T, TCA = c.T, c.TCA
    NT = T // TCA
    NSUB = TCA // 128
    base = car.off

    XTM = car.carve([128, NSUB, 1024], F32)
    LN0E = car.carve([128, 8, TCA + 1], F32)
    XX = car.carve([128, 8, TCA], F32)
    LNTM = car.carve([128, 1024], F32)
    LNSQ = LNTM
    KKSB = LNTM.bitcast(BF16).rearrange("p (a b) -> p a b", a=8)[:, :, 0:256]
    SLIV = car.carve([128, 16], F32)
    DUM = car.carve([128, 4], F32)
    XFA = car.carve([128, 8, TCA], BF16)
    XFB = XFA
    TALL = car.carve([128, TCA], BF16)
    U32 = car.carve([128, 2, TCA], F32)
    RSB = car.carve([128, 8, TCA], BF16)
    KSB = car.carve([128, 8, TCA], BF16)
    K2SB = car.carve([128, 8, TCA], BF16)
    KKNSB = car.carve([128, 8, TCA], BF16)
    ASIG = car.carve([128, 8, TCA], BF16)
    MASB = car.carve([128, 8, TCA], BF16)
    MK2SB = car.carve([128, 8, TCA], BF16)
    WNEG = car.carve([128, 8, TCA], F32)
    E2SB = car.carve([128, TCA], BF16)
    VTM = car.carve([128, NSUB, 1024], BF16)
    GTM = car.carve([128, NSUB, 1024], BF16)
    GT = car.carve([128, TCA], BF16)
    D1T = car.carve([64, TCA], BF16)
    L1A = car.carve([16, TCA], BF16)
    L1K = car.carve([16, TCA], BF16)
    L1M = car.carve([16, TCA], BF16)
    L1MK = car.carve([16, TCA], BF16)
    TP = [car.carve([128, TCA], BF16) for _ in range(4)]
    WM = car.carve([128, TCA], F32)
    EPR = WM
    TFS = car.carve([16, TCA], BF16)
    TTM = car.carve([128, NSUB, 16], F32)
    assert car.off <= car.total, car.off

    PS, PSN = c.PS, c.PSN
    COLS = c.COLS

    for ti in range(NT):
        tb = ti * TCA
        P.dma("qs", XTM[:, :, :], c.x_in[tb:tb + TCA, :].rearrange("(s p) f -> p s f", p=128),
              outs=["XTM"])
        if ti == 0:
            P.op("v", lambda v: v.memset(LN0E[:, :, 0:1], 0.0), outs=["LN0E"])
        else:
            P.op("v", lambda v: v.tensor_copy(LN0E[:, :, 0:1], LN0E[:, :, TCA:TCA + 1]),
                 ins=["LN0E"], outs=["LN0E"])
        for s4 in range(NSUB):
            P.op("v", lambda v, s4=s4: v.tensor_reduce(SLIV[:, 0:1], XTM[:, s4], AXX, ALU.add),
                 ins=["XTM"], outs=["SLIV"])
            P.op("v", lambda v, s4=s4: v.tensor_tensor(
                LNSQ[:, :], XTM[:, s4], XTM[:, s4], ALU.mult),
                 ins=["XTM"], outs=["LNTM"])
            P.op("v", lambda v: v.tensor_reduce(SLIV[:, 1:2], LNSQ[:, :], AXX, ALU.add),
                 ins=["LNTM"], outs=["SLIV"])
            P.op("v", lambda v: v.tensor_scalar(SLIV[:, 2:3], SLIV[:, 0:1], 1.0 / CM, None, ALU.mult),
                 ins=["SLIV"], outs=["SLIV"])
            P.op("v", lambda v: v.tensor_tensor(SLIV[:, 3:4], SLIV[:, 2:3], SLIV[:, 2:3], ALU.mult),
                 ins=["SLIV"], outs=["SLIV"])
            P.op("v", lambda v: v.tensor_scalar(SLIV[:, 4:5], SLIV[:, 1:2], 1.0 / CM, SLIV[:, 3:4],
                                                ALU.mult, ALU.subtract),
                 ins=["SLIV"], outs=["SLIV"])
            P.op("s", lambda s: s.activation(SLIV[:, 5:6], SLIV[:, 4:5], AF.Ln, bias=1e-5),
                 ins=["SLIV"], outs=["SLIV"])
            P.op("s", lambda s: s.activation(SLIV[:, 6:7], SLIV[:, 5:6], AF.Exp, scale=-0.5),
                 ins=["SLIV"], outs=["SLIV"])
            P.op("v", lambda v: v.scalar_tensor_tensor(SLIV[:, 7:8], SLIV[:, 2:3], -1.0, SLIV[:, 6:7],
                                                       ALU.mult, ALU.mult),
                 ins=["SLIV"], outs=["SLIV"])
            P.op("v", lambda v, s4=s4: v.tensor_scalar(LNTM[:, :], XTM[:, s4], SLIV[:, 6:7], SLIV[:, 7:8],
                                                       ALU.mult, ALU.add),
                 ins=["XTM", "SLIV"], outs=["LNTM"])
            for g2 in range(2):
                for j in range(4):
                    cc = g2 * 4 + j
                    P.pe(lambda t, cc=cc, j=j: t.transpose(PS[0][:, ts(j, 128)], LNTM[:, ts(cc, 128)],
                                                           c.IDENT[:, :]),
                         ins=["LNTM", "IDENT"], outs=[PSN[0]])
                P.op("s", lambda s, g2=g2, s4=s4: s.copy(
                    LN0E[:, g2 * 4:g2 * 4 + 4, 1 + s4 * 128:1 + s4 * 128 + 128],
                    PS[0][:, 0:512].rearrange("p (a b) -> p a b", a=4)),
                    ins=[PSN[0]], outs=["LN0E"])
        P.op("v", lambda v: v.tensor_tensor(XX[:, :, :], LN0E[:, :, 0:TCA], LN0E[:, :, 1:TCA + 1],
                                            ALU.subtract),
             ins=["LN0E"], outs=["XX"])
        for cc in range(8):
            P.op("v", lambda v, cc=cc: v.scalar_tensor_tensor(
                XFA[:, cc], XX[:, cc], COLS[:, cc, 0:1], LN0E[:, cc, 1:TCA + 1], ALU.mult, ALU.add),
                 ins=["XX", "COLS", "LN0E"], outs=["XF"])
        for cc in range(8):
            P.pe(lambda t, cc=cc: t.matmul(PS[1][:, 0:TCA], c.MAAW1[:, cc, :], XFA[:, cc],
                                           start=(cc == 0), stop=(cc == 7)),
                 ins=["MAAW1", "XF"], outs=[PSN[1]])
        P.op("s", lambda s: s.activation(TALL[0:64, :], PS[1][0:64, 0:TCA], AF.Tanh),
             ins=[PSN[1]], outs=["TALL"])
        P.op("s", lambda s: s.activation(TALL[64:128, :], PS[1][64:128, 0:TCA], AF.Tanh),
             ins=[PSN[1]], outs=["TALL"])

        def mix_f(f):
            src = TALL[32 * f:32 * f + 32, :]
            srcname = "TALL"
            for cc in range(8):
                pb = PS[2 + cc % 2]
                pn = PSN[2 + cc % 2]
                P.pe(lambda t, cc=cc, pb=pb, f=f, src=src: t.matmul(
                    pb[:, 0:TCA], c.W2ALL[32 * f:32 * f + 32, ts(cc, 128)], src,
                    start=True, stop=True, tile_position=(32 * f, 0)),
                     ins=["W2ALL", srcname], outs=[pn])
                P.op("v", lambda v, cc=cc, pb=pb, f=f: v.scalar_tensor_tensor(
                    U32[:, cc % 2], pb[:, 0:TCA], COLS[:, cc, 8 + f:9 + f], XX[:, cc],
                    ALU.add, ALU.mult),
                     ins=[pn, "XX", "COLS"], outs=["U32_%d" % (cc % 2)])
                P.op("g", lambda g, cc=cc: g.tensor_tensor(XFB[:, cc], U32[:, cc % 2],
                                                           LN0E[:, cc, 1:TCA + 1], ALU.add),
                     ins=["U32_%d" % (cc % 2), "LN0E"], outs=["XF"])

        def proj_fm(WT, wname, dst, dname, ia, ib):
            for fc in range(8):
                bi = ia if fc % 2 == 0 else ib
                pb, pn = PS[bi], PSN[bi]
                for cc in range(8):
                    P.pe(lambda t, fc=fc, cc=cc, pb=pb, WT=WT: t.matmul(
                        pb[:, 0:TCA], WT[:, cc, ts(fc, 128)], XFB[:, cc],
                        start=(cc == 0), stop=(cc == 7)),
                         ins=[wname, "XF"], outs=[pn])
                P.op("s", lambda s, fc=fc, pb=pb, dst=dst: s.copy(dst[:, fc], pb[:, 0:TCA]),
                     ins=[pn], outs=[dname])

        def lora1(W1, wname, dst, dname, mdim):
            for cc in range(8):
                P.pe(lambda t, cc=cc, W1=W1, mdim=mdim: t.matmul(
                    PS[1][0:mdim, 0:TCA], W1[:, cc, :], XFB[:, cc],
                    start=(cc == 0), stop=(cc == 7)),
                     ins=[wname, "XF"], outs=[PSN[1]])
            P.op("s", lambda s, dst=dst, mdim=mdim: s.activation(dst[:, :], PS[1][0:mdim, 0:TCA], AF.Tanh),
                 ins=[PSN[1]], outs=[dname])

        def lora1_copy(W1, wname, dst, dname, mdim):
            for cc in range(8):
                P.pe(lambda t, cc=cc, W1=W1, mdim=mdim: t.matmul(
                    PS[1][0:mdim, 0:TCA], W1[:, cc, :], XFB[:, cc],
                    start=(cc == 0), stop=(cc == 7)),
                     ins=[wname, "XF"], outs=[PSN[1]])
            P.op("s", lambda s, dst=dst, mdim=mdim: s.copy(dst[:, :], PS[1][0:mdim, 0:TCA]),
                 ins=[PSN[1]], outs=[dname])

        mix_f(0)
        proj_fm(c.WRT, "WRT", RSB, "RSB", 4, 5)
        for cc in range(8):
            P.pe(lambda t, cc=cc: t.matmul(PS[6][:, 0:TCA], c.GW1[:, cc, :], XFB[:, cc],
                                           start=(cc == 0), stop=(cc == 7)),
                 ins=["GW1", "XF"], outs=[PSN[6]])
        P.op("s", lambda s: s.activation(GT[:, :], PS[6][:, 0:TCA], AF.Tanh),
             ins=[PSN[6]], outs=["GT"])
        for sub in range(NSUB):
            for ch in range(2):
                P.pe(lambda t, sub=sub, ch=ch: t.matmul(
                    PS[7][:, 0:512], GT[:, ts(sub, 128)], c.GW2[:, ts(ch, 512)],
                    start=True, stop=True),
                     ins=["GT", "GW2"], outs=[PSN[7]])
                P.op("s", lambda s, sub=sub, ch=ch: s.copy(GTM[:, sub, ts(ch, 512)], PS[7][:, 0:512]),
                     ins=[PSN[7]], outs=["GTM"])

        mix_f(1)
        for cc in range(8):
            P.pe(lambda t, cc=cc: t.matmul(PS[6][0:64, 0:TCA], c.DW1[:, cc, :], XFB[:, cc],
                                           start=(cc == 0), stop=(cc == 7)),
                 ins=["DW1", "XF"], outs=[PSN[6]])
        P.op("s", lambda s: s.activation(D1T[:, :], PS[6][0:64, 0:TCA], AF.Tanh),
             ins=[PSN[6]], outs=["D1T"])
        lora1_copy(c.LA1, "LA1", L1A, "L1A", 16)
        lora1_copy(c.LMA1, "LMA1", L1M, "L1M", 16)
        for lsrc, lname, w2, w2n, dst, dname, ccol in (
                (L1A, "L1A", c.LA2, "LA2", ASIG, "ASIG", 2),
                (L1M, "L1M", c.LMA2, "LMA2", MASB, "MASB", 3)):
            for fc in range(8):
                pb, pn = (PS[6], PSN[6]) if fc % 2 == 0 else (PS[7], PSN[7])
                P.pe(lambda t, fc=fc, pb=pb, lsrc=lsrc, w2=w2: t.matmul(
                    pb[:, 0:TCA], w2[:, ts(fc, 128)], lsrc[:, :], start=True, stop=True),
                     ins=[w2n, lname], outs=[pn])
                P.op("s", lambda s, fc=fc, pb=pb, dst=dst, ccol=ccol: s.activation(
                    dst[:, fc], pb[:, 0:TCA], AF.Sigmoid, bias=COLS[:, fc, ccol:ccol + 1]),
                     ins=[pn, "COLS"], outs=[dname])

        mix_f(2)
        proj_fm(c.WKT, "WKT", KSB, "KSB", 4, 5)
        lora1(c.LKK1, "LKK1", L1K, "L1K", 16)
        lora1_copy(c.LMK1, "LMK1", L1MK, "L1MK", 16)
        for fc in range(8):
            pb, pn = (PS[6], PSN[6]) if fc % 2 == 0 else (PS[7], PSN[7])
            P.pe(lambda t, fc=fc, pb=pb: t.matmul(
                pb[:, 0:TCA], c.LKK2[:, ts(fc, 128)], L1K[:, :], start=True, stop=True),
                 ins=["LKK2", "L1K"], outs=[pn])
            P.op("v", lambda v, fc=fc, pb=pb: v.tensor_tensor(KKSB[:, fc], KSB[:, fc], pb[:, 0:TCA], ALU.add),
                 ins=[pn, "KSB"], outs=["LNTM"])
        for fc in range(8):
            pb, pn = (PS[6], PSN[6]) if fc % 2 == 0 else (PS[7], PSN[7])
            P.pe(lambda t, fc=fc, pb=pb: t.matmul(
                pb[:, 0:TCA], c.LMK2[:, ts(fc, 128)], L1MK[:, :], start=True, stop=True),
                 ins=["LMK2", "L1MK"], outs=[pn])
            P.op("s", lambda s, fc=fc, pb=pb: s.activation(
                MK2SB[:, fc], pb[:, 0:TCA], AF.Sigmoid, bias=COLS[:, fc, 4:5]),
                 ins=[pn, "COLS"], outs=["MK2SB"])

        mix_f(3)
        for sub in range(NSUB):
            for ch in range(2):
                for cc in range(8):
                    P.pe(lambda t, sub=sub, ch=ch, cc=cc: t.matmul(
                        PS[4][:, 0:512], XFB[:, cc, ts(sub, 128)], c.WVT[:, cc, ts(ch, 512)],
                        start=(cc == 0), stop=(cc == 7)),
                         ins=["XF", "WVT"], outs=[PSN[4]])
                P.op("s", lambda s, sub=sub, ch=ch: s.copy(VTM[:, sub, ts(ch, 512)], PS[4][:, 0:512]),
                     ins=[PSN[4]], outs=["VTM"])

        for fc in range(8):
            pb, pn = (PS[6], PSN[6]) if fc % 2 == 0 else (PS[7], PSN[7])
            P.pe(lambda t, fc=fc, pb=pb: t.matmul(
                pb[0:128, 0:TCA], c.DW2[:, ts(fc, 128)], D1T[:, :], start=True, stop=True),
                 ins=["DW2", "D1T"], outs=[pn])
            P.op("s", lambda s, fc=fc, pb=pb: s.activation(
                EPR[:, :], pb[:, 0:TCA], AF.Exp, scale=-1.0, bias=COLS[:, fc, 1:2]),
                 ins=[pn, "COLS"], outs=["WM"])
            P.op("s", lambda s, fc=fc: s.activation(
                WNEG[:, fc], EPR[:, :], AF.Ln, bias=1.6487212707001282),
                 ins=["WM"], outs=["WNEG"])
        for fc in range(8):
            P.op("v", lambda v, fc=fc: v.tensor_scalar(
                TP[0][:, :], MASB[:, fc], -1.0, 1.0, ALU.mult, ALU.add),
                 ins=["MASB"], outs=["TPa"])
            P.op("v", lambda v, fc=fc: v.tensor_tensor(
                TP[0][:, :], TP[0][:, :], ASIG[:, fc], ALU.mult),
                 ins=["TPa", "ASIG"], outs=["TPa"])
            P.op("g", lambda g, fc=fc: g.tensor_tensor(TP[1][:, :], TP[0][:, :], MASB[:, fc], ALU.add),
                 ins=["TPa", "MASB"], outs=["TPb"])
            P.op("v", lambda v, fc=fc: v.tensor_tensor(TP[2][:, :], KSB[:, fc], TP[1][:, :], ALU.mult),
                 ins=["KSB", "TPb"], outs=["TPc"])
            P.op("v", lambda v, fc=fc: v.scalar_tensor_tensor(
                WM[:, :], WNEG[:, fc], 0.0, MK2SB[:, fc], ALU.max, ALU.mult),
                 ins=["WNEG", "MK2SB"], outs=["WM"])
            P.op("s", lambda s, fc=fc: s.activation(E2SB[:, :], WM[:, :], AF.Exp, scale=-1.0),
                 ins=["WM"], outs=["E2SB"])
            P.op("v", lambda v, fc=fc: v.tensor_tensor(K2SB[:, fc], TP[2][:, :], E2SB[:, :], ALU.mult),
                 ins=["TPc", "E2SB"], outs=["K2SB"])
        for fc in range(8):
            P.op("s", lambda s, fc=fc: s.activation(TP[3][:, :], KKSB[:, fc], AF.Square),
                 ins=["LNTM"], outs=["TPd"])
            P.pe(lambda t, fc=fc: t.matmul(PS[1][0:16, 0:TCA], c.BONES[:, fc, :], TP[3][:, :],
                                           start=(fc == 0), stop=(fc == 7)),
                 ins=["BONES", "TPd"], outs=[PSN[1]])
        P.op("s", lambda s: s.activation(TFS[:, :], PS[1][0:16, 0:TCA], AF.Ln),
             ins=[PSN[1]], outs=["TFS"])
        P.op("s", lambda s: s.activation(TFS[:, :], TFS[:, :], AF.Exp, scale=-0.5),
             ins=["TFS"], outs=["TFS"])
        P.op("v", lambda v: v.tensor_scalar_min(TFS[:, :], TFS[:, :], 1e12),
             ins=["TFS"], outs=["TFS"])
        for fc in range(8):
            P.pe(lambda t, fc=fc: t.matmul(PS[2][:, 0:TCA], c.BCAST[:, fc, :], TFS[:, :],
                                           start=True, stop=True),
                 ins=["BCAST", "TFS"], outs=[PSN[2]])
            P.op("v", lambda v, fc=fc: v.tensor_tensor(KKNSB[:, fc], KKSB[:, fc], PS[2][:, 0:TCA], ALU.mult),
                 ins=[PSN[2], "LNTM"], outs=["KKNSB"])
        for fc in range(8):
            P.op("v", lambda v, fc=fc: v.tensor_tensor(TP[0][:, :], RSB[:, fc], K2SB[:, fc], ALU.mult),
                 ins=["RSB", "K2SB"], outs=["TPa"])
            P.pe(lambda t, fc=fc: t.matmul(PS[1][0:16, 0:TCA], c.BONESF[:, fc, :], TP[0][:, :],
                                           start=(fc == 0), stop=(fc == 7)),
                 ins=["BONESF", "TPa"], outs=[PSN[1]])
        P.op("s", lambda s: s.copy(TFS[:, :], PS[1][0:16, 0:TCA]), ins=[PSN[1]], outs=["TFS"])
        for sub in range(NSUB):
            P.pe(lambda t, sub=sub: t.transpose(PS[2].bitcast(BF16)[:, 0:16], TFS[:, ts(sub, 128)],
                                                c.IDENTB[0:16, 0:16]),
                 ins=["TFS", "IDENTB"], outs=[PSN[2]])
            P.op("s", lambda s, sub=sub: s.copy(TTM[:, sub, :], PS[2].bitcast(BF16)[:, 0:16]),
                 ins=[PSN[2]], outs=["TTM"])

        for dram, srcp, name in ((c.sp["sp_r"], RSB, "RSB"), (c.sp["sp_k2"], K2SB, "K2SB"),
                                 (c.sp["sp_kkn"], KKNSB, "KKNSB"), (c.sp["sp_asig"], ASIG, "ASIG")):
            P.dma("qs", dram[:, :, tb:tb + TCA].rearrange("cc p t -> p cc t"), srcp[:, :, :],
                  ins=[name])
        P.dma("qs", c.sp["sp_wneg"][:, :, tb:tb + TCA].rearrange("cc p t -> p cc t"),
              WNEG[:, :, :], ins=["WNEG"])
        P.dma("qs", c.sp["sp_v"][tb:tb + TCA, :].rearrange("(s p) f -> p s f", p=128), VTM[:, :, :],
              ins=["VTM"])
        P.dma("qs", c.sp["sp_g"][tb:tb + TCA, :].rearrange("(s p) f -> p s f", p=128), GTM[:, :, :],
              ins=["GTM"])
        P.dma("qs", c.sp["sp_term"][tb:tb + TCA, :].rearrange("(s p) f -> p s f", p=128), TTM[:, :, :],
              ins=["TTM"])
        for dnm, dsrc, dname in (("r", RSB, "RSB"), ("ln0", None, "LN0E"), ("k2", K2SB, "K2SB"),
                                 ("kkn", KKNSB, "KKNSB"), ("wneg", WNEG, "WNEG")):
            if dnm in c.dbg:
                srcap = LN0E[:, :, 1:TCA + 1] if dnm == "ln0" else dsrc[:, :, :]
                P.dma("qg", c.DBG[dnm][:, :, tb:tb + TCA].rearrange("cc p t -> p cc t"),
                      srcap, ins=[dname])
        if "v" in c.dbg:
            P.dma("qg", c.DBG["v"][tb:tb + TCA, :].rearrange("(s p) f -> p s f", p=128), VTM[:, :, :], ins=["VTM"])
        if "g" in c.dbg:
            P.dma("qg", c.DBG["g"][tb:tb + TCA, :].rearrange("(s p) f -> p s f", p=128), GTM[:, :, :], ins=["GTM"])
        if "term" in c.dbg:
            P.dma("qs", c.DBG["term"][tb:tb + TCA, :].rearrange("(s p) x -> p s x", p=128),
                  TTM[:, :, :], ins=["TTM"])
    P.barrier()
    c.p1a_end = car.off
    car.off = base

def emit_pass1b(P, c):
    if "1b" not in c.get("stages", ("1a", "1b", "2")):
        return
    emit_pass1b_real(P, c)


def emit_pass2(P, c):
    if "2" not in c.get("stages", ("1a", "1b", "2")):
        return
    emit_pass2_real(P, c)


def emit_pass1b_real(P, c):
    nc, car = P.nc, c.car
    T, TCB = c.T, c.TCB
    NT = T // TCB
    NCH = TCB // LCH
    NSUB = TCB // 128
    base = car.off
    car.off = c.after_wot

    RF = car.carve([128, 8, TCB], BF16)
    K2F = car.carve([128, 8, TCB], BF16)
    KKNF = car.carve([128, 8, TCB], BF16)
    ASF = car.carve([128, 8, TCB], BF16)
    WNF = car.carve([128, 8, TCB], F32)
    VT = car.carve([128, NSUB, 1024], BF16)
    GT2 = car.carve([128, NSUB, 1024], BF16)
    XT2 = car.carve([128, NSUB, 1024], F32)
    TERM2 = car.carve([128, NSUB, 16], F32)
    CWN = car.carve([128, 8, LCH], F32)
    AR1 = car.carve([128, 8, LCH], F32)
    AR2 = car.carve([128, 8, LCH], F32)
    BB = car.carve([128, 8, 8], F32)
    MH = car.carve([128, 8, 8], F32)
    BE = car.carve([128, 8, 8], F32)
    EE = car.carve([128, 8, LCH], BF16)
    EPV = car.carve([128, 8, LCH], BF16)
    EAV = car.carve([128, 8, LCH], BF16)
    EY1 = car.carve([128, 8, LCH], BF16)
    EOR = car.carve([128, 8, LCH], BF16)
    EOA = car.carve([128, 8, LCH], BF16)
    BSF = car.carve([128, 8, LCH], BF16)
    XDA = car.carve([128, 8, LCH], BF16)
    XDR = car.carve([128, 8, LCH], BF16)
    XOA = car.carve([128, 8, LCH], BF16)
    XOR_ = car.carve([128, 8, LCH], BF16)
    YDB = car.carve([128, 8, LCH], BF16)
    YDK = car.carve([128, 8, LCH], BF16)
    Y1B = car.carve([128, 8, 16 + LCH], BF16)
    Y1K = car.carve([128, 8, 16 + LCH], BF16)
    MRG = car.carve([128, 5, 128], BF16)
    BQC = car.carve([128, 512], BF16)
    QP = car.carve([128, 5, 128], BF16)
    PCUR = car.carve([128, 16, 64], BF16)
    PNXT = car.carve([128, 16, 64], BF16)
    CRS = car.carve([16, 16, 64], BF16)
    YCR = car.carve([16, 16, 64], F32)
    PTAIL = car.carve([16, 16, 64], BF16)
    VTAIL = car.carve([16, 16, 64], BF16)
    YTM = car.carve([128, NSUB, 1024], F32)
    GNS = car.carve([128, 64], F32)
    TPB0 = car.carve([128, 1024], BF16)
    TPF = car.carve([128, 1024], F32)
    TPF2 = car.carve([128, 1024], F32)
    YGF = car.carve([128, 8, 128], BF16)
    YF = car.carve([128, 1024], F32)
    X1 = car.carve([128, 1024], F32)
    SLB = car.carve([128, 16], F32)
    assert car.off <= car.total, car.off

    PS, PSN = c.PS, c.PSN
    P.op("v", lambda v: v.memset(PTAIL[:, :, :], 0.0), outs=["PTAIL"])
    P.op("v", lambda v: v.memset(VTAIL[:, :, :], 0.0), outs=["VTAIL"])
    P.op("v", lambda v: v.memset(Y1B[:, :, :], 0.0), outs=["Y1B"])
    P.op("v", lambda v: v.memset(Y1K[:, :, :], 0.0), outs=["Y1K"])

    NSTG = c.solve_levels

    def fm2(ap3):
        return ap3

    for ti in range(NT):
        tb = ti * TCB
        for dram, dst, name in ((c.sp["sp_r"], RF, "RF"), (c.sp["sp_k2"], K2F, "K2F"),
                                (c.sp["sp_kkn"], KKNF, "KKNF"), (c.sp["sp_asig"], ASF, "ASF")):
            P.dma("qs", dst[:, :, :], dram[:, :, tb:tb + TCB].rearrange("cc p t -> p cc t"), outs=[name])
        P.dma("qs", WNF[:, :, :], c.sp["sp_wneg"][:, :, tb:tb + TCB].rearrange("cc p t -> p cc t"), outs=["WNF"])
        P.dma("qs", VT[:, :, :], c.sp["sp_v"][tb:tb + TCB, :].rearrange("(s p) f -> p s f", p=128), outs=["VT"])
        P.dma("qs", GT2[:, :, :], c.sp["sp_g"][tb:tb + TCB, :].rearrange("(s p) f -> p s f", p=128), outs=["GT2"])
        P.dma("qs", XT2[:, :, :], c.x_in[tb:tb + TCB, :].rearrange("(s p) f -> p s f", p=128), outs=["XT2"])
        P.dma("qs", TERM2[:, :, :], c.sp["sp_term"][tb:tb + TCB, :].rearrange("(s p) f -> p s f", p=128),
              outs=["TERM2"])

        for ch in range(NCH):
            t0 = ch * LCH
            sub = (ti * TCB + t0) // 128
            ysub = t0 // 128
            P.op("v", lambda v: v.tensor_copy(Y1B[:, :, 0:16], Y1B[:, :, LCH:LCH + 16]),
                 ins=["Y1B"], outs=["Y1B"])
            P.op("v", lambda v: v.tensor_copy(Y1K[:, :, 0:16], Y1K[:, :, LCH:LCH + 16]),
                 ins=["Y1K"], outs=["Y1K"])
            for cc in range(8):
                P.op("v", lambda v, cc=cc, t0=t0: v.tensor_tensor_scan(
                    CWN[:, cc], WNF[:, cc, t0:t0 + LCH], WNF[:, cc, t0:t0 + LCH], 0.0,
                    ALU.add, ALU.bypass),
                     ins=["WNF"], outs=["CWN"])
            P.op("v", lambda v: v.tensor_copy(MH[:, :, :], CWN[:, :, 7::16]), ins=["CWN"], outs=["MH"])
            P.op("v", lambda v: v.tensor_copy(BE[:, :, :], CWN[:, :, 15::16]), ins=["CWN"], outs=["BE"])
            P.op("v", lambda v: v.tensor_copy(BB[:, :, 1:8], BE[:, :, 0:7]), ins=["BE"], outs=["BB"])
            P.op("v", lambda v: v.memset(BB[:, :, 0:1], 0.0), outs=["BB"])
            P.op("v", lambda v: v.tensor_tensor(
                AR1.rearrange("p c (b k) -> p c b k", k=16),
                MH.unsqueeze(3).broadcast_to([128, 8, 8, 16]),
                CWN.rearrange("p c (b k) -> p c b k", k=16), ALU.subtract),
                 ins=["MH", "CWN"], outs=["AR1"])
            P.op("s", lambda s: s.activation(fm2(EE), fm2(AR1), AF.Exp), ins=["AR1"], outs=["EE"])
            P.op("s", lambda s: s.activation(fm2(EPV), fm2(AR1), AF.Exp, scale=-1.0), ins=["AR1"], outs=["EPV"])
            P.op("v", lambda v, t0=t0: v.tensor_tensor(fm2(AR2), fm2(AR1),
                                                       fm2(WNF[:, :, t0:t0 + LCH]), ALU.add),
                 ins=["AR1", "WNF"], outs=["AR2"])
            P.op("s", lambda s: s.activation(fm2(EAV), fm2(AR2), AF.Exp), ins=["AR2"], outs=["EAV"])
            P.op("v", lambda v: v.tensor_tensor(
                AR1.rearrange("p c (b k) -> p c b k", k=16),
                CWN.rearrange("p c (b k) -> p c b k", k=16),
                BE.unsqueeze(3).broadcast_to([128, 8, 8, 16]), ALU.subtract),
                 ins=["CWN", "BE"], outs=["AR1"])
            P.op("s", lambda s: s.activation(fm2(EY1), fm2(AR1), AF.Exp), ins=["AR1"], outs=["EY1"])
            P.op("v", lambda v: v.tensor_tensor(
                AR1.rearrange("p c (b k) -> p c b k", k=16),
                BB.unsqueeze(3).broadcast_to([128, 8, 8, 16]),
                CWN.rearrange("p c (b k) -> p c b k", k=16), ALU.subtract),
                 ins=["BB", "CWN"], outs=["AR1"])
            P.op("s", lambda s: s.activation(fm2(EOR), fm2(AR1), AF.Exp), ins=["AR1"], outs=["EOR"])
            P.op("v", lambda v, t0=t0: v.tensor_tensor(fm2(AR2), fm2(AR1),
                                                       fm2(WNF[:, :, t0:t0 + LCH]), ALU.add),
                 ins=["AR1", "WNF"], outs=["AR2"])
            P.op("s", lambda s: s.activation(fm2(EOA), fm2(AR2), AF.Exp), ins=["AR2"], outs=["EOA"])

            def tt(dst, dn, a, an, b, bn, neg=False):
                if neg:
                    P.op("v", lambda v: v.scalar_tensor_tensor(fm2(dst), fm2(a), -1.0, fm2(b),
                                                               ALU.mult, ALU.mult),
                         ins=[an, bn], outs=[dn])
                else:
                    P.op("v", lambda v: v.tensor_tensor(fm2(dst), fm2(a), fm2(b), ALU.mult),
                         ins=[an, bn], outs=[dn])
            rfc = RF[:, :, t0:t0 + LCH]; k2c = K2F[:, :, t0:t0 + LCH]
            kkc = KKNF[:, :, t0:t0 + LCH]; asc = ASF[:, :, t0:t0 + LCH]
            tt(BSF, "BSF", kkc, "KKNF", asc, "ASF")
            tt(XDR, "XDR", rfc, "RF", EE, "EE")
            tt(XDA, "XDA", kkc, "KKNF", EAV, "EAV", neg=True)
            tt(XOR_, "XOR", rfc, "RF", EOR, "EOR")
            tt(XOA, "XOA", kkc, "KKNF", EOA, "EOA", neg=True)
            tt(YDB, "YDB", BSF, "BSF", EPV, "EPV")
            tt(YDK, "YDK", k2c, "K2F", EPV, "EPV")
            tt(Y1B[:, :, 16:16 + LCH], "Y1B", BSF, "BSF", EY1, "EY1")
            tt(Y1K[:, :, 16:16 + LCH], "Y1K", k2c, "K2F", EY1, "EY1")

            for h in range(16):
                cc = h // 2
                lo = (h % 2) * 64
                hsl = slice(lo, lo + 64)
                tp = (lo, 0)

                def hs(ap3, cc=cc, hsl=hsl):
                    return ap3[hsl, cc, :]

                bq0, bq0n = PS[0], PSN[0]
                bq1, bq1n = PS[1], PSN[1]
                bq2, bq2n = PS[2], PSN[2]
                pw1, pw1n = PS[3], PSN[3]
                pw2, pw2n = PS[4], PSN[4]
                y1b_cur = Y1B[hsl, cc, 16:16 + LCH]
                y1k_cur = Y1K[hsl, cc, 16:16 + LCH]
                for lhsT, ln_, rhs, rn_, qt in ((hs(YDB), "YDB", hs(XDA), "XDA", 0),
                                                (hs(XDA), "XDA", hs(YDB), "YDB", 1),
                                                (hs(YDK), "YDK", hs(XDA), "XDA", 2),
                                                (hs(YDB), "YDB", hs(XDR), "XDR", 3)):
                    P.pe(lambda t, lhsT=lhsT, rhs=rhs, qt=qt, bq0=bq0, tp=tp: t.matmul(
                        bq0[:, ts(qt, 128)], lhsT, rhs, start=True, stop=True, tile_position=tp),
                         ins=[ln_, rn_], outs=[bq0n])
                for lhsT, ln_, rhs, rn_, qt in ((y1b_cur, "Y1B", hs(XOA), "XOA", 0),
                                                (hs(XOA), "XOA", y1b_cur, "Y1B", 1),
                                                (y1k_cur, "Y1K", hs(XOA), "XOA", 2),
                                                (y1b_cur, "Y1B", hs(XOR_), "XOR", 3)):
                    P.pe(lambda t, lhsT=lhsT, rhs=rhs, qt=qt, bq1=bq1, tp=tp: t.matmul(
                        bq1[:, ts(qt, 128)], lhsT, rhs, start=True, stop=True, tile_position=tp),
                         ins=[ln_, rn_], outs=[bq1n])
                for lhsT, ln_, rhs, rn_, qt in ((hs(YDK), "YDK", hs(XDR), "XDR", 0),
                                                (y1k_cur, "Y1K", hs(XOR_), "XOR", 1)):
                    P.pe(lambda t, lhsT=lhsT, rhs=rhs, qt=qt, bq2=bq2, tp=tp: t.matmul(
                        bq2[:, ts(qt, 128)], lhsT, rhs, start=True, stop=True, tile_position=tp),
                         ins=[ln_, rn_], outs=[bq2n])
                for lhsT, ln_, rhs, rn_, j in ((Y1B[hsl, cc, 0:16], "Y1B", XOA[hsl, cc, 0:16], "XOA", 0),
                                               (Y1K[hsl, cc, 0:16], "Y1K", XOA[hsl, cc, 0:16], "XOA", 1),
                                               (Y1B[hsl, cc, 0:16], "Y1B", XOR_[hsl, cc, 0:16], "XOR", 2),
                                               (Y1K[hsl, cc, 0:16], "Y1K", XOR_[hsl, cc, 0:16], "XOR", 3)):
                    P.pe(lambda t, lhsT=lhsT, rhs=rhs, j=j, bq2=bq2, tp=tp: t.matmul(
                        bq2[0:16, 256 + j * 16:256 + j * 16 + 16], lhsT, rhs,
                        start=True, stop=True, tile_position=tp),
                         ins=[ln_, rn_], outs=[bq2n])
                P.op("s", lambda s, h=h, bq2=bq2: s.copy(CRS[:, h, :], bq2[0:16, 256:320]),
                     ins=[bq2n], outs=["CRS"])
                mt = MRG
                P.op("v", lambda v, mt=mt, bq1=bq1: v.scalar_tensor_tensor(
                    mt[:, 0:4].rearrange("p a b -> p (a b)"), bq1[:, 0:512], 1.0,
                    c.MSKB1C[:, :], ALU.mult, ALU.mult),
                     ins=[bq1n, "MSKB1C"], outs=["MRG"])
                P.op("s", lambda s, bq0=bq0: s.copy(BQC[:, :], bq0[:, 0:512]),
                     ins=[bq0n], outs=["BQC"])
                P.op("v", lambda v, mt=mt: v.copy_predicated(
                    mt[:, 0:4].rearrange("p a b -> p (a b)").bitcast(U16),
                    c.MSKD0CB[:, :].bitcast(U16), BQC[:, :].bitcast(U16)),
                     ins=["BQC", "MSKD0CB", "MRG"], outs=["MRG"])
                P.op("v", lambda v, mt=mt, bq2=bq2: v.scalar_tensor_tensor(
                    mt[:, 4], bq2[:, ts(1, 128)], 1.0, c.MSK_ST_B1[:, :], ALU.mult, ALU.mult),
                     ins=[bq2n, "MSK_ST_B1"], outs=["MRG"])
                P.op("v", lambda v, bq2=bq2: v.tensor_copy(BQC[:, 0:128], bq2[:, ts(0, 128)]),
                     ins=[bq2n], outs=["BQC"])
                P.op("v", lambda v, mt=mt: v.copy_predicated(
                    mt[:, 4].bitcast(U16), c.MSK_ST_IB[:, :].bitcast(U16),
                    BQC[:, 0:128].bitcast(U16)),
                     ins=["BQC", "MSK_ST_IB", "MRG"], outs=["MRG"])
                ab_st = mt[:, 0]; ab_ts = mt[:, 1]
                qh = QP
                P.pe(lambda t, ab_st=ab_st, ab_ts=ab_ts, pw1=pw1: t.matmul(
                    pw1[:, 0:128], ab_st, ab_ts, start=True, stop=True), ins=["MRG"], outs=[pw1n])
                P.pe(lambda t, ab_st=ab_st, ab_ts=ab_ts, pw1=pw1: t.matmul(
                    pw1[:, 128:256], ab_ts, ab_st, start=True, stop=True), ins=["MRG"], outs=[pw1n])
                P.op("s", lambda s, qh=qh, pw1=pw1: s.copy(qh[:, 0:2].rearrange("p a b -> p (a b)"), pw1[:, 0:256]),
                     ins=[pw1n], outs=["QP"])
                P.pe(lambda t, qh=qh, pw1=pw1: t.matmul(pw1[:, 256:384], qh[:, 1], qh[:, 0],
                                               start=True, stop=True), ins=["QP"], outs=[pw1n])
                P.pe(lambda t, qh=qh, pw1=pw1: t.matmul(pw1[:, 384:512], qh[:, 0], qh[:, 1],
                                               start=True, stop=True), ins=["QP"], outs=[pw1n])
                P.op("s", lambda s, qh=qh, pw1=pw1: s.copy(qh[:, 2:4].rearrange("p a b -> p (a b)"), pw1[:, 256:512]),
                     ins=[pw1n], outs=["QP"])
                P.pe(lambda t, qh=qh, pw2=pw2: t.matmul(pw2[:, 0:128], qh[:, 3], qh[:, 2],
                                               start=True, stop=True), ins=["QP"], outs=[pw2n])
                P.op("s", lambda s, qh=qh, pw2=pw2: s.copy(qh[:, 4], pw2[:, 0:128]),
                     ins=[pw2n], outs=["QP"])
                vsl = VT[:, ysub, h * 64:(h + 1) * 64]
                P.pe(lambda t, mt=mt, vsl=vsl, pw2=pw2: t.matmul(pw2[:, 128:192], mt[:, 2], vsl,
                                                        start=True, stop=True), ins=["MRG", "VT"], outs=[pw2n])
                P.pe(lambda t, h=h, pw2=pw2: t.matmul(pw2[0:16, 448:512], CRS[:, h, 0:16],
                                             PTAIL[:, h, :], start=True, stop=False),
                     ins=["CRS", "PTAIL"], outs=[pw2n])
                P.pe(lambda t, h=h, pw2=pw2: t.matmul(pw2[0:16, 448:512], CRS[:, h, 16:32],
                                             VTAIL[:, h, :], start=False, stop=True),
                     ins=["CRS", "VTAIL"], outs=[pw2n])
                P.op("s", lambda s, h=h, pw2=pw2: s.copy(PCUR[:, h, :], pw2[:, 128:192]),
                     ins=[pw2n], outs=["PCUR"])
                P.op("v", lambda v, h=h, pw2=pw2: v.tensor_tensor(
                    PCUR[0:16, h, :], PCUR[0:16, h, :], pw2[0:16, 448:512], ALU.add),
                     ins=[pw2n, "PCUR"], outs=["PCUR"])
                pcur, pnxt = PCUR, PNXT
                pcn, pnn = "PCUR", "PNXT"
                qidx = [None, 1, 3, 4]
                for st in range(NSTG):
                    qsl = mt[:, 0] if st == 0 else qh[:, qidx[st]]
                    qname = "MRG" if st == 0 else "QP"
                    off = 192 + (st % 2) * 64
                    P.pe(lambda t, qsl=qsl, pcur=pcur, h=h, off=off, pw2=pw2: t.matmul(
                        pw2[:, off:off + 64], qsl, pcur[:, h, :], start=True, stop=False),
                         ins=[qname, pcn], outs=[pw2n])
                    P.pe(lambda t, pcur=pcur, h=h, off=off, pw2=pw2: t.matmul(
                        pw2[:, off:off + 64], c.IDENTB[:, :], pcur[:, h, :], start=False, stop=True),
                         ins=["IDENTB", pcn], outs=[pw2n])
                    P.op("s", lambda s, pnxt=pnxt, h=h, off=off, pw2=pw2: s.copy(pnxt[:, h, :], pw2[:, off:off + 64]),
                         ins=[pw2n], outs=[pnn])
                    pcur, pnxt = pnxt, pcur
                    pcn, pnn = pnn, pcn
                yb = PS[6] if h < 8 else PS[7]
                ybn = PSN[6] if h < 8 else PSN[7]
                yoff = (h % 8) * 64
                P.pe(lambda t, mt=mt, pcur=pcur, h=h, yb=yb, yoff=yoff: t.matmul(
                    yb[:, yoff:yoff + 64], mt[:, 3], pcur[:, h, :], start=True, stop=False),
                     ins=["MRG", pcn], outs=[ybn])
                P.pe(lambda t, mt=mt, vsl=vsl, yb=yb, yoff=yoff: t.matmul(
                    yb[:, yoff:yoff + 64], mt[:, 4], vsl, start=False, stop=True),
                     ins=["MRG", "VT"], outs=[ybn])
                P.pe(lambda t, h=h, pw2=pw2: t.matmul(
                    pw2[0:16, 384:448], CRS[:, h, 32:48], PTAIL[:, h, :], start=True, stop=False),
                     ins=["CRS", "PTAIL"], outs=[pw2n])
                P.pe(lambda t, h=h, pw2=pw2: t.matmul(
                    pw2[0:16, 384:448], CRS[:, h, 48:64], VTAIL[:, h, :], start=False, stop=True),
                     ins=["CRS", "VTAIL"], outs=[pw2n])
                P.op("s", lambda s, h=h, pw2=pw2: s.copy(YCR[:, h, :], pw2[0:16, 384:448]),
                     ins=[pw2n], outs=["YCR"])
                P.pe(lambda t, pcur=pcur, h=h, pw2=pw2: t.matmul(
                    pw2[0:16, 320:384], c.TAILX[:, :], pcur[:, h, :], start=True, stop=True),
                     ins=["TAILX", pcn], outs=[pw2n])
                P.op("s", lambda s, h=h, pw2=pw2: s.copy(PTAIL[:, h, :], pw2[0:16, 320:384]),
                     ins=[pw2n], outs=["PTAIL"])
            P.op("s", lambda s, ysub=ysub: s.copy(YTM[:, ysub, 0:512], PS[6][:, 0:512]),
                 ins=[PSN[6]], outs=["YTM"])
            P.op("s", lambda s, ysub=ysub: s.copy(YTM[:, ysub, 512:1024], PS[7][:, 0:512]),
                 ins=[PSN[7]], outs=["YTM"])
            P.op("v", lambda v, ysub=ysub: v.tensor_tensor(
                YTM[0:16, ysub, :], YTM[0:16, ysub, :],
                YCR.rearrange("p a b -> p (a b)"), ALU.add),
                 ins=["YTM", "YCR"], outs=["YTM"])
            for rnd in range(2):
                for q in range(2):
                    P.pe(lambda t, ysub=ysub, rnd=rnd, q=q: t.matmul(
                        PS[5][0:16, ts(q, 256)], c.TAILX[:, :],
                        VT[:, ysub, rnd * 512 + q * 256: rnd * 512 + (q + 1) * 256],
                        start=True, stop=True),
                         ins=["TAILX", "VT"], outs=[PSN[5]])
                P.op("s", lambda s, rnd=rnd: s.copy(
                    VTAIL.rearrange("p a b -> p (a b)")[:, ts(rnd, 512)], PS[5][0:16, 0:512]),
                     ins=[PSN[5]], outs=["VTAIL"])

        for su in range(NSUB):
            ysl = YTM[:, su, :]
            P.op("s", lambda s, ysl=ysl: s.activation(TPB0[:, :], ysl, AF.Square),
                 ins=["YTM"], outs=["TPB0"])
            P.op("v", lambda v, ysl=ysl: v.tensor_reduce(
                GNS[:, 0:16], ysl.rearrange("p (h n) -> p h n", n=64), AXX, ALU.add),
                 ins=["YTM"], outs=["GNS"])
            P.op("v", lambda v: v.tensor_reduce(
                GNS[:, 16:32], TPB0.rearrange("p (h n) -> p h n", n=64), AXX, ALU.add),
                 ins=["TPB0"], outs=["GNS"])
            P.op("v", lambda v: v.tensor_scalar(GNS[:, 32:48], GNS[:, 0:16], 1.0 / ND, None, ALU.mult),
                 ins=["GNS"], outs=["GNS"])
            P.op("v", lambda v: v.tensor_tensor(SLB[:, 0:16], GNS[:, 32:48], GNS[:, 32:48], ALU.mult),
                 ins=["GNS"], outs=["SLB"])
            P.op("v", lambda v: v.tensor_scalar(GNS[:, 48:64], GNS[:, 16:32], 1.0 / ND, None, ALU.mult),
                 ins=["GNS"], outs=["GNS"])
            P.op("v", lambda v: v.tensor_tensor(GNS[:, 48:64], GNS[:, 48:64], SLB[:, 0:16], ALU.subtract),
                 ins=["GNS", "SLB"], outs=["GNS"])
            P.op("s", lambda s: s.activation(GNS[:, 48:64], GNS[:, 48:64], AF.Ln, bias=0.00064),
                 ins=["GNS"], outs=["GNS"])
            P.op("s", lambda s: s.activation(GNS[:, 48:64], GNS[:, 48:64], AF.Exp, scale=-0.5),
                 ins=["GNS"], outs=["GNS"])
            P.op("v", lambda v, ysl=ysl: v.tensor_tensor(
                TPF.rearrange("p (h n) -> p h n", n=64), ysl.rearrange("p (h n) -> p h n", n=64),
                GNS[:, 32:48].unsqueeze(2).broadcast_to([128, 16, 64]), ALU.subtract),
                 ins=["YTM", "GNS"], outs=["TPF"])
            P.op("v", lambda v: v.tensor_tensor(
                TPF.rearrange("p (h n) -> p h n", n=64), TPF.rearrange("p (h n) -> p h n", n=64),
                GNS[:, 48:64].unsqueeze(2).broadcast_to([128, 16, 64]), ALU.mult),
                 ins=["TPF", "GNS"], outs=["TPF"])
            P.op("v", lambda v: v.scalar_tensor_tensor(TPF[:, :], TPF[:, :], 1.0, c.LNWR[:, :],
                                                       ALU.mult, ALU.mult),
                 ins=["TPF", "LNWR"], outs=["TPF"])
            P.op("v", lambda v: v.tensor_tensor(TPF[:, :], TPF[:, :], c.LNBR[:, :], ALU.add),
                 ins=["TPF", "LNBR"], outs=["TPF"])
            P.op("v", lambda v, su=su: v.tensor_tensor(
                TPF2.rearrange("p (h n) -> p h n", n=64), VT[:, su, :].rearrange("p (h n) -> p h n", n=64),
                TERM2[:, su, :].unsqueeze(2).broadcast_to([128, 16, 64]), ALU.mult),
                 ins=["VT", "TERM2"], outs=["TPF2"])
            P.op("v", lambda v: v.tensor_tensor(TPF[:, :], TPF[:, :], TPF2[:, :], ALU.add),
                 ins=["TPF", "TPF2"], outs=["TPF"])
            P.op("v", lambda v, su=su: v.tensor_tensor(TPB0[:, :], TPF[:, :], GT2[:, su, :], ALU.mult),
                 ins=["TPF", "GT2"], outs=["TPB0"])
            for g2 in range(2):
                for j in range(4):
                    ccx = g2 * 4 + j
                    P.pe(lambda t, ccx=ccx, j=j: t.transpose(
                        PS[1].bitcast(BF16)[:, ts(j, 128)], TPB0[:, ts(ccx, 128)], c.IDENTB[:, :]),
                         ins=["TPB0", "IDENTB"], outs=[PSN[1]])
                P.op("s", lambda s, g2=g2: s.copy(
                    YGF[:, g2 * 4:g2 * 4 + 4, :], PS[1].bitcast(BF16)[:, 0:512].rearrange("p (a b) -> p a b", a=4)),
                    ins=[PSN[1]], outs=["YGF"])
            for ch2 in range(2):
                pb, pn = PS[2 + ch2], PSN[2 + ch2]
                for ccx in range(8):
                    P.pe(lambda t, ccx=ccx, ch2=ch2, pb=pb: t.matmul(
                        pb[:, 0:512], YGF[:, ccx, :], c.WOT[:, ccx, ts(ch2, 512)],
                        start=(ccx == 0), stop=(ccx == 7)),
                         ins=["YGF", "WOT"], outs=[pn])
                P.op("s", lambda s, ch2=ch2, pb=pb: s.copy(YF[:, ts(ch2, 512)], pb[:, 0:512]),
                     ins=[pn], outs=["YF"])
            P.op("v", lambda v: v.tensor_reduce(SLB[:, 0:1], YF[:, :], AXX, ALU.add),
                 ins=["YF"], outs=["SLB"])
            P.op("v", lambda v: v.tensor_tensor(TPF[:, :], YF[:, :], YF[:, :], ALU.mult),
                 ins=["YF"], outs=["TPF"])
            P.op("v", lambda v: v.tensor_reduce(SLB[:, 1:2], TPF[:, :], AXX, ALU.add),
                 ins=["TPF"], outs=["SLB"])
            P.op("v", lambda v: v.tensor_scalar(SLB[:, 2:3], SLB[:, 0:1], 1.0 / CM, None, ALU.mult),
                 ins=["SLB"], outs=["SLB"])
            P.op("v", lambda v: v.tensor_tensor(SLB[:, 3:4], SLB[:, 2:3], SLB[:, 2:3], ALU.mult),
                 ins=["SLB"], outs=["SLB"])
            P.op("v", lambda v: v.tensor_scalar(SLB[:, 4:5], SLB[:, 1:2], 1.0 / CM, SLB[:, 3:4],
                                                ALU.mult, ALU.subtract),
                 ins=["SLB"], outs=["SLB"])
            P.op("s", lambda s: s.activation(SLB[:, 5:6], SLB[:, 4:5], AF.Ln, bias=1e-5),
                 ins=["SLB"], outs=["SLB"])
            P.op("s", lambda s: s.activation(SLB[:, 6:7], SLB[:, 5:6], AF.Exp, scale=-0.5),
                 ins=["SLB"], outs=["SLB"])
            P.op("v", lambda v: v.scalar_tensor_tensor(SLB[:, 7:8], SLB[:, 2:3], -1.0, SLB[:, 6:7],
                                                       ALU.mult, ALU.mult),
                 ins=["SLB"], outs=["SLB"])
            P.op("v", lambda v: v.tensor_scalar(X1[:, :], YF[:, :], SLB[:, 6:7], SLB[:, 7:8],
                                                ALU.mult, ALU.add),
                 ins=["YF", "SLB"], outs=["X1"])
            P.op("v", lambda v, su=su: v.tensor_tensor(X1[:, :], X1[:, :], XT2[:, su, :], ALU.add),
                 ins=["X1", "XT2"], outs=["X1"])
            P.dma("qs", c.sp["x1s"][tb + su * 128:tb + (su + 1) * 128, :], X1[:, :], ins=["X1"])
            if "x1" in c.dbg:
                P.dma("qs", c.DBG["x1"][tb + su * 128:tb + (su + 1) * 128, :], X1[:, :], ins=["X1"])
            if "ytm" in c.dbg:
                P.dma("qs", c.DBG["ytm"][tb + su * 128:tb + (su + 1) * 128, :], YTM[:, su, :], ins=["YTM"])
    P.barrier()
    car.off = base


def emit_pass2_real(P, c):
    nc, car = P.nc, c.car
    T, TCF = c.T, min(c.TCF, c.T)
    NT = T // TCF
    SUBF = min(512, TCF)
    NGRP = TCF // SUBF
    NS128 = SUBF // 128
    base = car.off
    car.off = c.const_end

    WOFN = car.carve([128, 32, 1024], BF16)
    W1B = [car.carve([128, 8, 128], BF16) for _ in range(4)]
    X1T = car.carve([128, TCF // 128, 1024], F32)
    LN1TM = car.carve([128, TCF // 128, 1024], F32)
    LN1F = car.carve([128, 8, TCF], BF16)
    INTER = car.carve([128, 8 * (4096 // 1024), SUBF], BF16)
    GE = car.carve([128, 4, SUBF], BF16)
    MSKE = car.carve([128, 3, SUBF], BF16)
    GEM = car.carve([128, 8], F32)
    RL4 = car.carve([4, 128], BF16)
    MASKR = car.carve([1, 384], BF16)
    TPF3 = car.carve([128, 1024], F32)
    YF2 = car.carve([128, 1024], F32)
    OUT2 = car.carve([128, 1024], F32)
    SLF = car.carve([128, 16], F32)
    assert car.off <= car.total, car.off

    PS, PSN = c.PS, c.PSN

    P.dma("qg", WOFN[:, :, :], c.W["Wo"].rearrange("(a p) f -> p a f", p=128), outs=["WOFN"])

    for ti in range(NT):
        tb = ti * TCF
        P.dma("qs", X1T[:, :, :], c.sp["x1s"][tb:tb + TCF, :].rearrange("(s p) f -> p s f", p=128),
              outs=["X1T"])
        for su in range(TCF // 128):
            P.op("v", lambda v, su=su: v.tensor_reduce(SLF[:, 0:1], X1T[:, su], AXX, ALU.add),
                 ins=["X1T"], outs=["SLF"])
            P.op("v", lambda v, su=su: v.tensor_tensor(
                TPF3[:, :], X1T[:, su], X1T[:, su], ALU.mult),
                 ins=["X1T"], outs=["TPF3"])
            P.op("v", lambda v: v.tensor_reduce(SLF[:, 1:2], TPF3[:, :], AXX, ALU.add),
                 ins=["TPF3"], outs=["SLF"])
            P.op("v", lambda v: v.tensor_scalar(SLF[:, 2:3], SLF[:, 0:1], 1.0 / CM, None, ALU.mult),
                 ins=["SLF"], outs=["SLF"])
            P.op("v", lambda v: v.tensor_tensor(SLF[:, 3:4], SLF[:, 2:3], SLF[:, 2:3], ALU.mult),
                 ins=["SLF"], outs=["SLF"])
            P.op("v", lambda v: v.tensor_scalar(SLF[:, 4:5], SLF[:, 1:2], 1.0 / CM, SLF[:, 3:4],
                                                ALU.mult, ALU.subtract),
                 ins=["SLF"], outs=["SLF"])
            P.op("s", lambda s: s.activation(SLF[:, 5:6], SLF[:, 4:5], AF.Ln, bias=1e-5),
                 ins=["SLF"], outs=["SLF"])
            P.op("s", lambda s: s.activation(SLF[:, 6:7], SLF[:, 5:6], AF.Exp, scale=-0.5),
                 ins=["SLF"], outs=["SLF"])
            P.op("v", lambda v: v.scalar_tensor_tensor(SLF[:, 7:8], SLF[:, 2:3], -1.0, SLF[:, 6:7],
                                                       ALU.mult, ALU.mult),
                 ins=["SLF"], outs=["SLF"])
            P.op("v", lambda v, su=su: v.tensor_scalar(LN1TM[:, su, :], X1T[:, su], SLF[:, 6:7],
                                                       SLF[:, 7:8], ALU.mult, ALU.add),
                 ins=["X1T", "SLF"], outs=["LN1TM"])
            for g2 in range(2):
                for j in range(4):
                    ccx = g2 * 4 + j
                    P.pe(lambda t, ccx=ccx, j=j, su=su: t.transpose(
                        PS[0][:, ts(j, 128)], LN1TM[:, su, ts(ccx, 128)], c.IDENT[:, :]),
                         ins=["LN1TM", "IDENT"], outs=[PSN[0]])
                P.op("s", lambda s, g2=g2, su=su: s.copy(
                    LN1F[:, g2 * 4:g2 * 4 + 4, su * 128:(su + 1) * 128],
                    PS[0][:, 0:512].rearrange("p (a b) -> p a b", a=4)),
                    ins=[PSN[0]], outs=["LN1F"])
            for cc in range(8):
                P.pe(lambda t, cc=cc, su=su: t.matmul(
                    PS[1][0:4, 0:128], c.WROUT[:, cc, :], LN1F[:, cc, su * 128:(su + 1) * 128],
                    start=(cc == 0), stop=(cc == 7)),
                     ins=["WROUT", "LN1F"], outs=[PSN[1]])
            P.op("s", lambda s: s.copy(RL4[:, :], PS[1][0:4, 0:128]), ins=[PSN[1]], outs=["RL4"])
            P.pe(lambda t: t.transpose(PS[1].bitcast(BF16)[:, 0:4], RL4[:, :], c.IDENTB[0:4, 0:4]),
                 ins=["RL4", "IDENTB"], outs=[PSN[1]])
            P.op("s", lambda s: s.copy(GEM[:, 0:4], PS[1].bitcast(BF16)[:, 0:4]),
                 ins=[PSN[1]], outs=["GEM"])
            P.op("v", lambda v: v.tensor_reduce(GEM[:, 4:5], GEM[:, 0:4], AXX, ALU.max),
                 ins=["GEM"], outs=["GEM"])
            P.op("v", lambda v: v.tensor_scalar(GEM[:, 5:8], GEM[:, 0:3], GEM[:, 4:5], None, ALU.is_ge),
                 ins=["GEM"], outs=["GEM"])
            P.op("v", lambda v: v.tensor_tensor(GEM[:, 4:5], GEM[:, 5:6], GEM[:, 6:7], ALU.max),
                 ins=["GEM"], outs=["GEM"])
            for j, colsrc in ((0, 5), (1, 7), (2, 4)):
                P.pe(lambda t, j=j, colsrc=colsrc: t.transpose(
                    PS[1][0:1, ts(j, 128)], GEM[:, colsrc:colsrc + 1], c.IDENT[:, :]),
                     ins=["GEM", "IDENT"], outs=[PSN[1]])
            P.op("s", lambda s: s.copy(MASKR[0:1, :], PS[1][0:1, 0:384]), ins=[PSN[1]], outs=["MASKR"])
            for j in range(3):
                P.pe(lambda t, j=j: t.matmul(
                    PS[1][:, ts(j, 128)], c.ONESROW[0:1, 0:128], MASKR[0:1, ts(j, 128)],
                    start=True, stop=True),
                     ins=["ONESROW", "MASKR"], outs=[PSN[1]])
            P.op("v", lambda v, su=su: v.tensor_copy(
                MSKE[:, :, (su % NS128) * 128:(su % NS128) * 128 + 128],
                PS[1][:, 0:384].rearrange("p (a b) -> p a b", a=3)),
                 ins=[PSN[1]], outs=["MSKE"])
            if (su + 1) % NS128 == 0:
                gi = su // NS128
                tok0 = gi * SUBF
                for ic in range(8):
                    for isub in range(4):
                        ii0 = ic * 4 + isub
                        for e in range(4):
                            wb = W1B[e]
                            wbn = "W1B%d" % e
                            P.dma("qg", wb[:, :, :],
                                  c.W["W1"][e, :, ic * 512 + isub * 128: ic * 512 + (isub + 1) * 128]
                                  .rearrange("(a p) i -> p a i", p=128),
                                  outs=[wbn])
                            pb, pn = PS[2 + e % 2], PSN[2 + e % 2]
                            for cc in range(8):
                                P.pe(lambda t, cc=cc, pb=pb, wb=wb, tok0=tok0: t.matmul(
                                    pb[:, 0:SUBF], wb[:, cc, :],
                                    LN1F[:, cc, tok0:tok0 + SUBF],
                                    start=(cc == 0), stop=(cc == 7)),
                                     ins=[wbn, "LN1F"], outs=[pn])
                            P.op("s", lambda s, e=e, isub=isub, pb=pb, ic=ic: s.activation(
                                GE[:, e, :], pb[:, 0:SUBF], AF.Gelu,
                                bias=c.B1COL[:, e, ic * 4 + isub:ic * 4 + isub + 1]),
                                 ins=[pn, "B1COL"], outs=["GE%d" % e])
                            if e == 3:
                                ii = ic * 4 + isub
                                P.op("v", lambda v, ii=ii: v.tensor_copy(INTER[:, ii, :], GE[:, 3, :]),
                                     ins=["GE3"], outs=["INTER"])
                                P.op("v", lambda v, ii=ii: v.copy_predicated(
                                    INTER[:, ii, :].bitcast(U16), MSKE[:, 1, :].bitcast(U16),
                                    GE[:, 2, :].bitcast(U16)),
                                     ins=["GE2", "MSKE", "INTER"], outs=["INTER"])
                                P.op("v", lambda v, ii=ii: v.copy_predicated(
                                    INTER[:, ii, :].bitcast(U16), MSKE[:, 2, :].bitcast(U16),
                                    GE[:, 1, :].bitcast(U16)),
                                     ins=["GE1", "MSKE", "INTER"], outs=["INTER"])
                                P.op("v", lambda v, ii=ii: v.copy_predicated(
                                    INTER[:, ii, :].bitcast(U16), MSKE[:, 0, :].bitcast(U16),
                                    GE[:, 0, :].bitcast(U16)),
                                     ins=["GE0", "MSKE", "INTER"], outs=["INTER"])
                for tsub in range(NS128):
                    gsu = gi * NS128 + tsub
                    for ch in range(2):
                        pb, pn = PS[4 + ch], PSN[4 + ch]
                        P.pe(lambda t, ch=ch, pb=pb: t.matmul(
                            pb[:, 0:512], c.ONESROW[0:1, 0:128], c.BOROW[0:1, ts(ch, 512)],
                            start=True, stop=False),
                             ins=["ONESROW", "BOROW"], outs=[pn])
                        for icc in range(32):
                            P.pe(lambda t, icc=icc, ch=ch, tsub=tsub, pb=pb: t.matmul(
                                pb[:, 0:512], INTER[:, icc, ts(tsub, 128)], WOFN[:, icc, ts(ch, 512)],
                                start=False, stop=(icc == 31)),
                                 ins=["INTER", "WOFN"], outs=[pn])
                        P.op("v", lambda v, ch=ch, gsu=gsu, pb=pb: v.tensor_tensor(
                            YF2[:, ts(ch, 512)], pb[:, 0:512], LN1TM[:, gsu, ts(ch, 512)], ALU.add),
                             ins=[pn, "LN1TM"], outs=["YF2"])
                    P.op("v", lambda v: v.tensor_reduce(SLF[:, 0:1], YF2[:, :], AXX, ALU.add),
                         ins=["YF2"], outs=["SLF"])
                    P.op("v", lambda v: v.tensor_tensor(TPF3[:, :], YF2[:, :], YF2[:, :], ALU.mult),
                         ins=["YF2"], outs=["TPF3"])
                    P.op("v", lambda v: v.tensor_reduce(SLF[:, 1:2], TPF3[:, :], AXX, ALU.add),
                         ins=["TPF3"], outs=["SLF"])
                    P.op("v", lambda v: v.tensor_scalar(SLF[:, 2:3], SLF[:, 0:1], 1.0 / CM, None, ALU.mult),
                         ins=["SLF"], outs=["SLF"])
                    P.op("v", lambda v: v.tensor_tensor(SLF[:, 3:4], SLF[:, 2:3], SLF[:, 2:3], ALU.mult),
                         ins=["SLF"], outs=["SLF"])
                    P.op("v", lambda v: v.tensor_scalar(SLF[:, 4:5], SLF[:, 1:2], 1.0 / CM, SLF[:, 3:4],
                                                        ALU.mult, ALU.subtract),
                         ins=["SLF"], outs=["SLF"])
                    P.op("s", lambda s: s.activation(SLF[:, 5:6], SLF[:, 4:5], AF.Ln, bias=1e-12),
                         ins=["SLF"], outs=["SLF"])
                    P.op("s", lambda s: s.activation(SLF[:, 6:7], SLF[:, 5:6], AF.Exp, scale=-0.5),
                         ins=["SLF"], outs=["SLF"])
                    P.op("v", lambda v: v.scalar_tensor_tensor(SLF[:, 7:8], SLF[:, 2:3], -1.0, SLF[:, 6:7],
                                                               ALU.mult, ALU.mult),
                         ins=["SLF"], outs=["SLF"])
                    P.op("v", lambda v: v.tensor_scalar(TPF3[:, :], YF2[:, :], SLF[:, 6:7], SLF[:, 7:8],
                                                        ALU.mult, ALU.add),
                         ins=["YF2", "SLF"], outs=["TPF3"])
                    P.op("v", lambda v: v.scalar_tensor_tensor(TPF3[:, :], TPF3[:, :], 1.0, c.GAMR[:, :],
                                                               ALU.mult, ALU.mult),
                         ins=["TPF3", "GAMR"], outs=["TPF3"])
                    P.op("v", lambda v: v.tensor_tensor(TPF3[:, :], TPF3[:, :], c.BETR[:, :], ALU.add),
                         ins=["TPF3", "BETR"], outs=["TPF3"])
                    P.op("v", lambda v: v.tensor_reduce(SLF[:, 0:1], TPF3[:, :], AXX, ALU.add),
                         ins=["TPF3"], outs=["SLF"])
                    P.op("v", lambda v: v.tensor_tensor(YF2[:, :], TPF3[:, :], TPF3[:, :], ALU.mult),
                         ins=["TPF3"], outs=["YF2"])
                    P.op("v", lambda v: v.tensor_reduce(SLF[:, 1:2], YF2[:, :], AXX, ALU.add),
                         ins=["YF2"], outs=["SLF"])
                    P.op("v", lambda v: v.tensor_scalar(SLF[:, 2:3], SLF[:, 0:1], 1.0 / CM, None, ALU.mult),
                         ins=["SLF"], outs=["SLF"])
                    P.op("v", lambda v: v.tensor_tensor(SLF[:, 3:4], SLF[:, 2:3], SLF[:, 2:3], ALU.mult),
                         ins=["SLF"], outs=["SLF"])
                    P.op("v", lambda v: v.tensor_scalar(SLF[:, 4:5], SLF[:, 1:2], 1.0 / CM, SLF[:, 3:4],
                                                        ALU.mult, ALU.subtract),
                         ins=["SLF"], outs=["SLF"])
                    P.op("s", lambda s: s.activation(SLF[:, 5:6], SLF[:, 4:5], AF.Ln, bias=1e-5),
                         ins=["SLF"], outs=["SLF"])
                    P.op("s", lambda s: s.activation(SLF[:, 6:7], SLF[:, 5:6], AF.Exp, scale=-0.5),
                         ins=["SLF"], outs=["SLF"])
                    P.op("v", lambda v: v.scalar_tensor_tensor(SLF[:, 7:8], SLF[:, 2:3], -1.0, SLF[:, 6:7],
                                                               ALU.mult, ALU.mult),
                         ins=["SLF"], outs=["SLF"])
                    P.op("v", lambda v: v.tensor_scalar(OUT2[:, :], TPF3[:, :], SLF[:, 6:7],
                                                        SLF[:, 7:8], ALU.mult, ALU.add),
                         ins=["TPF3", "SLF"], outs=["OUT2"])
                    P.op("v", lambda v, gsu=gsu: v.tensor_tensor(
                        OUT2[:, :], OUT2[:, :], X1T[:, gsu, :], ALU.add),
                         ins=["OUT2", "X1T"], outs=["OUT2"])
                    P.dma("qs", c.out_t[tb + gsu * 128:tb + (gsu + 1) * 128, :], OUT2[:, :], ins=["OUT2"])
    P.barrier()
    car.off = base


def kernel(**inputs):
    x = np.ascontiguousarray(np.asarray(inputs["L0_attn_x"], np.float32))
    B = x.shape[0]
    wmap = {}
    for k, v in inputs.items():
        if k == "L0_attn_x":
            continue
        short = k.replace("L0_attn_", "").replace("L0_ffn_", "")
        wmap[short] = np.ascontiguousarray(np.asarray(v, np.float32))
    consts = {k: np.ascontiguousarray(np.asarray(v, np.float32)) for k, v in host_consts().items()}
    nc, c = build_core_program(T=4096)
    in_maps = []
    for b in range(B):
        m = {"x": np.ascontiguousarray(x[b])}
        m.update(wmap)
        m.update(consts)
        in_maps.append(m)
    res = run_bass_kernel_spmd(nc, in_maps, core_ids=list(range(NCORES)))
    return np.stack([r["out"] for r in res.results]).astype(np.float32)
